# version 08
# baseline (speedup 1.0000x reference)
import numpy as np
import ml_dtypes
from contextlib import ExitStack

import concourse.bass as bass
import concourse.mybir as mybir
from concourse.bass_utils import run_bass_kernel_spmd

F32 = mybir.dt.float32
BF16 = mybir.dt.bfloat16
FP16 = mybir.dt.float16
AF = mybir.ActivationFunctionType
ALU = mybir.AluOpType
AX = mybir.AxisListType

T, D, H, HD = 4096, 1024, 16, 64
NQB, NKB, NTB, NHP = 8, 32, 32, 8
XW = 65
XTB = H * XW
NCORE = 8

SEMS = [
    "m_xl", "m_il", "m_wld", "m_wst", "m_wrl", "m_xin", "m_out", "m_sp",
    "s_prep", "s_xc", "s_pro", "s_wpc",
    "s_sc", "s_ex", "s_pc", "s_pv", "s_pve",
    "s_xft", "s_xfe", "s_zt", "s_epr", "s_ept2", "s_eps", "s_ln",
    "s_mix", "s_fp", "s_fu", "s_wc", "s_ft", "s_fz", "s_l2",
]

XF2SLOT = {0: 0, 7: 3}
for _q in range(1, 7):
    XF2SLOT[_q] = 1 + (_q - 1) % 2


class W:

    def __init__(self, role, eng, tn, sem, cfg=None):
        self.role = role
        self.e = eng
        self.t = tn
        self.sem = sem
        self.cfg = cfg or {}
        self.c = {name: 0 for name in SEMS}
        self.last = None
        self.xfe_done = {}
        self.pro_end = 0

    def on(self, role):
        return self.role == role

    def w(self, role, sname, val):
        if self.role == role and val > 0:
            self.e.wait_ge(self.sem[sname], val)

    def i(self, role, sname, n=1):
        self.c[sname] += n
        if self.role == role:
            assert self.last is not None, f"inc {sname} with no instruction"
            self.last.then_inc(self.sem[sname], n)

    def em(self, ins):
        self.last = ins
        return ins

    def walk(self):
        n_hp = self.cfg.get("n_hp", NQB * NHP)
        do_ep = self.cfg.get("do_ep", True)
        do_ffn = self.cfg.get("do_ffn", True)
        self.prologue()
        self.pro_end = self.c["s_pro"]
        if n_hp <= 0:
            self.finale()
            return
        self.build_xf_serial(0, 0)
        done = 0
        complete = True
        for qb in range(NQB):
            for hp in range(NHP):
                if done >= n_hp:
                    complete = False
                    break
                self.hp_phase(qb, hp)
                done += 1
            if not complete:
                break
            if do_ep:
                self.epilogue(qb)
                if do_ffn and qb >= 1:
                    self.ffn(qb)
        if complete and do_ep and do_ffn:
            self.ffn_zero()
        self.finale()

    def prologue(self):
        t, c = self.t, self.c
        if self.on("gpsimd"):
            g = self.e
            for idt, n in ((t["idb"], 128), (t["idf"], 128)):
                g.memset(idt[:], 0.0)
                g.affine_select(out=idt[:], in_=idt[:], compare_op=ALU.not_equal,
                                fill=1.0, base=0, pattern=[[-1, n]],
                                channel_multiplier=1)
            g.memset(t["scl_i"][:], 0.125)
            g.memset(t["m1"][:], -1.0)
            g.memset(t["epsap"][:], 1e-5)
            g.memset(t["inv1024"][:], 1.0 / 1024.0)
            g.memset(t["mhalf"][:], -0.5)
            g.memset(t["one1"][:], 1.0)
            g.memset(t["onesf"], 1.0)
            g.memset(t["m8row"], -0.125)
            xtones = t["xt"].ap().rearrange("p (tb h w) -> p tb h w", h=H, w=XW)
            self.em(g.memset(xtones[:, :, :, 64:65], 1.0))
        self.i("gpsimd", "s_prep", 1)

        xst = t["pvraw"]
        for tb in range(NTB):
            sl = tb % 2
            self.w("sync", "s_xc", tb - 1)
            if self.on("sync"):
                self.em(self.e.dma_start(out=xst[:, sl * 1024:(sl + 1) * 1024],
                                         in_=t["dx"].ap()[tb * 128:(tb + 1) * 128, :]))
            self.i("sync", "m_xl", 16)
            self.w("vector", "m_xl", 16 * (tb + 1))
            self.w("vector", "s_prep", 1)
            if self.on("vector"):
                dst = t["xt"].ap().rearrange("p (tb h w) -> p tb h w", h=H, w=XW)
                src = xst[:, sl * 1024:(sl + 1) * 1024].rearrange(
                    "p (h d) -> p h d", d=HD)
                self.em(self.e.tensor_copy(dst[:, tb, :, 0:64], src))
            self.i("vector", "s_xc", 1)

        n2 = t["s1"][:, 0:512]
        sq = t["xin"][:, 0:1024]
        if self.on("vector"):
            v = self.e
            xtv = t["xt"].ap().rearrange("p (tb h w) -> p tb h w", h=H, w=XW)
            sq3 = sq.rearrange("p (h d) -> p h d", d=HD)
            for tb in range(NTB):
                vsl = xtv[:, tb, :, 0:64]
                v.tensor_mul(sq3, vsl, vsl)
                ins = v.reduce_sum(n2[:, tb * 16:(tb + 1) * 16], sq3, axis=AX.X)
            self.em(ins)
        self.i("vector", "s_pro", 1)
        self.w("vector", "s_pro", c["s_pro"])
        if self.on("vector"):
            self.em(self.e.reduce_max(t["c128"][:], n2, axis=AX.X))
        self.i("vector", "s_pro", 1)
        self.w("sync", "s_pro", c["s_pro"])
        if self.on("sync"):
            self.em(self.e.dma_start(out=t["dcrow"].ap(), in_=t["c128"][:]))
        self.i("sync", "m_il", 16)
        self.w("sync", "m_il", c["m_il"])
        if self.on("sync"):
            self.em(self.e.dma_start(out=t["crow"], in_=t["dcrow"].ap()))
        self.i("sync", "m_il", 16)
        if self.on("sync"):
            self.em(self.e.dma_start(out=t["ipw1"][:], in_=t["dipw"].ap()))
        self.i("sync", "m_il", 16)
        for dname, base in (("dmur", 0), ("dmuk", 16)):
            for ch in range(8):
                if self.on("sync"):
                    src = self.t[dname].ap().rearrange("(c p) -> c p", p=128)
                    self.em(self.e.dma_start(out=t["mu2"][:, base + ch:base + ch + 1],
                                             in_=src[ch:ch + 1, :]))
                self.i("sync", "m_il", 16)
        self.w("vector", "m_il", c["m_il"])
        if self.on("vector"):
            self.em(self.e.reduce_max(t["c11"][0:1, 0:1], t["crow"], axis=AX.X))
        self.i("vector", "s_pro", 1)
        if self.on("scalar"):
            from concourse.hw_specs import get_activation_tables
            names = list(get_activation_tables(self.e.bass.m.arch).keys())
            idx = names.index("natural_log_exp_and_others")
            nc0 = self.e.bass
            self.e.add_instruction(mybir.InstLoadActFuncSet(
                name=nc0.get_next_instruction_name(),
                act_func_set_id=idx, ins=[], outs=[]))
        self.w("scalar", "m_il", c["m_il"])
        self.w("scalar", "s_prep", 1)
        if self.on("scalar"):
            self.em(self.e.activation(out=t["wsc"][:], in_=t["ipw1"][:],
                                      func=AF.Exp, bias=0.0, scale=t["m1"][:]))
        self.i("scalar", "s_pro", 1)
        self.w("vector", "s_pro", c["s_pro"])
        if self.on("vector"):
            self.em(self.e.tensor_copy(t["lna"][:, 0:1], t["wsc"][:]))
        self.i("vector", "s_pro", 1)
        self.w("scalar", "s_pro", c["s_pro"])
        if self.on("scalar"):
            sc = self.e
            sc.activation(out=t["lna"][:, 1:2], in_=t["lna"][:, 0:1], func=AF.Ln,
                          bias=t["one1"][:], scale=1.0)
            self.em(sc.activation(out=t["wsc"][:], in_=t["lna"][:, 1:2],
                                  func=AF.Exp, bias=0.0, scale=t["m1"][:]))
        self.i("scalar", "s_pro", 1)
        self.w("vector", "s_pro", c["s_pro"])
        if self.on("vector"):
            self.em(self.e.tensor_scalar(out=t["wsc"][:], in0=t["wsc"][:],
                                         scalar1=0.125, scalar2=None,
                                         op0=ALU.mult))
        self.i("vector", "s_pro", 1)
        self.w("tensor", "s_pro", c["s_pro"])
        self.w("tensor", "s_prep", 1)
        if self.on("tensor"):
            self.em(self.e.matmul(t["psB7"][:, 128:129], t["m8row"],
                                  t["c11"][0:1, 0:1], start=True, stop=True))
        self.i("tensor", "s_pro", 1)
        self.w("vector", "s_pro", c["s_pro"])
        if self.on("vector"):
            v = self.e
            v.tensor_copy(t["negc"][:], t["psB7"][:, 128:129])
            v.tensor_scalar(out=t["mu2"][:, 8:16], in0=t["mu2"][:, 0:8],
                            scalar1=-1.0, scalar2=1.0, op0=ALU.mult, op1=ALU.add)
            self.em(v.tensor_scalar(out=t["mu2"][:, 24:32], in0=t["mu2"][:, 16:24],
                                    scalar1=-1.0, scalar2=1.0, op0=ALU.mult,
                                    op1=ALU.add))
        self.i("vector", "s_pro", 1)

        wst = t["pvraw"]
        wbf = t["wring"]
        j = 0
        for dname, wi in (("dwr", 0), ("dwk", 1), ("dwv", 2)):
            for dm in range(8):
                sl = j % 2
                self.w("gpsimd", "s_wpc", j - 1)
                if self.on("gpsimd"):
                    self.em(self.e.dma_start(
                        out=wst[:, 2048 + sl * 1024:2048 + (sl + 1) * 1024],
                        in_=t[dname].ap()[dm * 128:(dm + 1) * 128, :]))
                self.i("gpsimd", "m_wld", 16)
                self.w("vector", "m_wld", 16 * (j + 1))
                self.w("vector", "m_wst", 16 * (j - 1))
                if self.on("vector"):
                    self.em(self.e.tensor_copy(
                        wbf[:, sl * 1024:(sl + 1) * 1024],
                        wst[:, 2048 + sl * 1024:2048 + (sl + 1) * 1024]))
                self.i("vector", "s_wpc", 1)
                self.w("gpsimd", "s_wpc", j + 1)
                if self.on("gpsimd"):
                    off = wi * (8 * 128 * 8 * 128) + dm * 128
                    dst = bass.AP(t["dwscr"], off,
                                  [[8 * 128, 128], [128 * 8 * 128, 8], [1, 128]])
                    self.em(self.e.dma_start(
                        out=dst, in_=wbf[:, sl * 1024:(sl + 1) * 1024]))
                self.i("gpsimd", "m_wst", 16)
                j += 1

    def psxf_group(self, tposes, evac):
        g = self.c["s_xft"]
        half = g % 2
        self.w("tensor", "s_xfe", g)
        if self.on("tensor"):
            self.em(tposes(self.e, half))
        self.i("tensor", "s_xft", 1)
        self.w("vector", "s_xft", g + 1)
        if self.on("vector"):
            self.em(evac(self.e, half))
        self.i("vector", "s_xfe", 1)

    def _xf_tposes_evac(self, B, hp, g):
        t = self.t
        sl = B % 2
        h0, h1 = 2 * hp, 2 * hp + 1

        def tposes(eng, half):
            ins = None
            for k in range(4):
                tb = g * 4 + k
                eng.transpose(t["psXF"][0:64, half * 512 + k * 128:
                                        half * 512 + (k + 1) * 128],
                              t["xt"][:, tb * XTB + h0 * XW:tb * XTB + h0 * XW + 64],
                              t["idb"][:])
                ins = eng.transpose(t["psXF"][64:128, half * 512 + k * 128:
                                              half * 512 + (k + 1) * 128],
                                    t["xt"][:, tb * XTB + h1 * XW:
                                            tb * XTB + h1 * XW + 64],
                                    t["idb"][:], tile_position=(0, 64))
            return ins

        def evac(eng, half):
            return eng.tensor_copy(
                t["xf"][:, sl * 4096 + g * 512:sl * 4096 + (g + 1) * 512],
                t["psXF"][:, half * 512:(half + 1) * 512])

        return tposes, evac

    def build_xf_serial(self, qb, hp):
        B = qb * NHP + hp
        self.w("tensor", "s_xc", NTB)
        self.w("tensor", "s_prep", 1)
        self.w("vector", "s_sc", (B - 1) * NKB)
        for g in range(8):
            tposes, evac = self._xf_tposes_evac(B, hp, g)
            self.psxf_group(tposes, evac)
        self.xfe_done[B] = self.c["s_xfe"]

    def build_xf_group(self, qb, hp, g):
        B = qb * NHP + hp
        if g == 0:
            self.w("vector", "s_sc", (B - 1) * NKB)
        tposes, evac = self._xf_tposes_evac(B, hp, g)
        self.psxf_group(tposes, evac)
        if g == 7:
            self.xfe_done[B] = self.c["s_xfe"]

    def hp_phase(self, qb, hp):
        t = self.t
        B = qb * NHP + hp
        sl = B % 2
        xfb = sl * 4096
        n_kb = self.cfg.get("n_kb", NKB)

        nxt = None
        if B + 1 < NQB * NHP:
            nxt = divmod(B + 1, NHP)

        for kb in range(n_kb):
            g = B * NKB + kb
            p = kb % 2
            self.w("tensor", "s_ex", g - 1)
            if kb == 0:
                self.w("tensor", "s_xfe", self.xfe_done[B])
                self.w("tensor", "s_eps", self.c["s_eps"])
            if self.on("tensor"):
                tn = self.e
                tn.matmul(t["psS"][:, p * 1024:p * 1024 + 512],
                          t["xf"][0:64, xfb + kb * 128:xfb + (kb + 1) * 128],
                          t["xf"][0:64, xfb + qb * 512:xfb + (qb + 1) * 512],
                          start=True, stop=True, tile_position=(0, 0))
                self.em(tn.matmul(t["psS"][:, p * 1024 + 512:(p + 1) * 1024],
                                  t["xf"][64:128, xfb + kb * 128:xfb + (kb + 1) * 128],
                                  t["xf"][64:128, xfb + qb * 512:xfb + (qb + 1) * 512],
                                  start=True, stop=True, tile_position=(64, 0)))
            self.i("tensor", "s_sc", 1)

            if kb >= 1:
                self.pv_pair(qb, hp, kb - 1)

            self.w("scalar", "s_sc", g + 1)
            self.w("scalar", "s_pc", g - 2)
            if g == 0:
                self.w("scalar", "s_pro", self.pro_end)
            intra = (kb // 4) == qb
            if self.on("scalar"):
                scale = t["scl_i"][:] if intra else t["wsc"][:]
                self.em(self.e.activation(
                    out=t["pun"][:, (g % 3) * 1024:(g % 3 + 1) * 1024],
                    in_=t["psS"][:, p * 1024:(p + 1) * 1024],
                    func=AF.Exp, bias=t["negc"][:], scale=scale))
            self.i("scalar", "s_ex", 1)

            if nxt is not None and 2 <= kb < 18 and (kb - 2) % 2 == 0:
                self.build_xf_group(nxt[0], nxt[1], (kb - 2) // 2)

        if n_kb < NKB:
            return
        self.pv_pair(qb, hp, NKB - 1)
        if self.cfg.get("no_evac"):
            return

        self.w("vector", "s_pc", (B + 1) * NKB)
        emask = self.cfg.get("evac_mask", 15)
        if self.on("vector"):
            v = self.e
            ins = v.engine_nop()
            if emask & 1:
                ins = v.tensor_copy(t["pvraw"][0:64, hp * 512:(hp + 1) * 512],
                                    t["psPV0"][0:64, :])
            if emask & 2:
                ins = v.tensor_copy(t["pvraw"][64:128, hp * 512:(hp + 1) * 512],
                                    t["psPV1"][0:64, :])
            if emask & 4:
                ins = v.tensor_copy(t["zst2"][0:1, 0:512], t["psPV0"][64:65, :])
            if emask & 8:
                ins = v.tensor_copy(t["zst2"][0:1, 512:1024], t["psPV1"][64:65, :])
            self.em(ins)
        self.i("vector", "s_pve", 1)
        if self.cfg.get("no_ztr"):
            return

        self.w("tensor", "s_pve", B + 1)
        self.w("tensor", "s_epr", qb)
        self.w("tensor", "s_fu", self.c["s_fu"])
        if self.on("tensor"):
            tn = self.e
            ins = None
            for hh, base in ((2 * hp, 0), (2 * hp + 1, 512)):
                for ts in range(4):
                    ins = tn.transpose(
                        t["psB7"][:, ts * 16 + hh:ts * 16 + hh + 1],
                        t["zst2"][0:1, base + ts * 128:base + (ts + 1) * 128],
                        t["idf"][0:1, 0:1])
            self.em(ins)
        self.i("tensor", "s_zt", 1)

    def pv_pair(self, qb, hp, kb):
        t = self.t
        B = qb * NHP + hp
        g = B * NKB + kb
        h0, h1 = 2 * hp, 2 * hp + 1
        self.w("tensor", "s_ex", g + 1)
        if kb == 0:
            self.w("tensor", "s_pve", B)
        if self.on("tensor"):
            tn = self.e
            start, stop = kb == 0, kb == NKB - 1
            tn.matmul(t["psPV0"][0:65, :],
                      t["xt"][:, kb * XTB + h0 * XW:kb * XTB + (h0 + 1) * XW],
                      t["pun"][:, (g % 3) * 1024:(g % 3) * 1024 + 512],
                      start=start, stop=stop)
            self.em(tn.matmul(t["psPV1"][0:65, :],
                              t["xt"][:, kb * XTB + h1 * XW:kb * XTB + (h1 + 1) * XW],
                              t["pun"][:, (g % 3) * 1024 + 512:(g % 3 + 1) * 1024],
                              start=start, stop=stop))
        self.i("tensor", "s_pc", 1)

    def epilogue(self, qb):
        t = self.t
        eps = self.cfg.get("ep_stage", 9)
        self.w("vector", "m_out", self.c["m_out"])
        if qb == 1:
            self.w("vector", "m_sp", 16)
        self.w("scalar", "s_zt", (qb + 1) * NHP)
        if self.on("scalar"):
            sc = self.e
            sc.activation(out=t["fstage"][:, 512:576], in_=t["psB7"][:, 0:64],
                          func=AF.Ln, bias=0.0, scale=1.0)
            self.em(sc.activation(out=t["recipz"][:], in_=t["fstage"][:, 512:576],
                                  func=AF.Exp, bias=0.0, scale=t["m1"][:]))
        self.i("scalar", "s_epr", 1)

        if eps < 1:
            return
        xdst = t["pvraw"] if self.cfg.get("xin_to_pvraw") else t["xin"]
        nxin = self.cfg.get("n_xin", 4)
        for ts in range(4):
            if ts >= nxin:
                continue
            self.w("sync", "m_out", self.c["m_out"])
            self.w("sync", "s_ln", self.c["s_ln"])
            if self.on("sync"):
                row = qb * 512 + ts * 128
                self.em(self.e.dma_start(
                    out=xdst[:, ts * 1024:(ts + 1) * 1024],
                    in_=t["dx"].ap()[row:row + 128, :]))
            self.i("sync", "m_xin", 16)

        if eps < 2:
            return
        for ts in range(4):
            for hp in range(NHP):
                a = self.c["s_ept2"]
                col = (a % 4) * 512 + ((a // 4) % 4) * 128
                self.w("tensor", "s_eps", a - 2)
                if ts == 0 and hp == 0:
                    self.w("tensor", "s_ex", (qb + 1) * NHP * NKB)
                    self.w("tensor", "s_pve", (qb + 1) * NHP)
                if self.on("tensor"):
                    self.em(self.e.transpose(
                        t["psS"][:, col:col + 128],
                        t["pvraw"][:, hp * 512 + ts * 128:hp * 512 + (ts + 1) * 128],
                        t["idf"][:]))
                self.i("tensor", "s_ept2", 1)
                self.w("vector", "s_ept2", a + 1)
                if ts == 0 and hp == 0:
                    self.w("vector", "s_epr", qb + 1)
                    self.w("vector", "m_xin", self.c["m_xin"])
                if self.on("vector"):
                    v = self.e
                    h0, h1 = 2 * hp, 2 * hp + 1
                    base = ts * 1024 + hp * 128
                    v.scalar_tensor_tensor(
                        out=t["s1"][:, base:base + 64],
                        in0=t["psS"][:, col:col + 64],
                        scalar=t["recipz"][:, ts * 16 + h0:ts * 16 + h0 + 1],
                        in1=t["xin"][:, base:base + 64],
                        op0=ALU.mult, op1=ALU.add)
                    self.em(v.scalar_tensor_tensor(
                        out=t["s1"][:, base + 64:base + 128],
                        in0=t["psS"][:, col + 64:col + 128],
                        scalar=t["recipz"][:, ts * 16 + h1:ts * 16 + h1 + 1],
                        in1=t["xin"][:, base + 64:base + 128],
                        op0=ALU.mult, op1=ALU.add))
                self.i("vector", "s_eps", 1)
            if qb == self.cfg.get("dump_y_qb", -1):
                self.w("sync", "s_eps", self.c["s_eps"])
                if self.on("sync"):
                    self.em(self.e.dma_start(
                        out=bass.AP(self.t["dbg_ydump"], ts * 128 * 1024,
                                    [[1024, 128], [1, 1024]]),
                        in_=t["s1"][:, ts * 1024:(ts + 1) * 1024]))
                self.i("sync", "m_sp", 16)
                self.w("vector", "m_sp", self.c["m_sp"])
            if eps >= 3:
                self.ln(t["s1"], ts, first=True)
            if eps >= 4:
                self.xf2_build(qb, ts)

        if qb == 0 and eps >= 5:
            self.w("gpsimd", "s_ln", self.c["s_ln"])
            if self.on("gpsimd"):
                self.em(self.e.dma_start(out=t["ds1scr"].ap(), in_=t["s1"][:]))
            self.i("gpsimd", "m_sp", 16)

    def ln(self, buf, ts, first):
        t = self.t
        sname = "s_ln" if first else "s_l2"
        y = buf[:, ts * 1024:(ts + 1) * 1024]
        ebuf = t["xin"][:, ts * 1024:(ts + 1) * 1024] if first else \
            t["fstage"][:, 0:1024]
        if self.on("vector"):
            self.em(self.e.reduce_sum(t["lna"][:, 0:1], y, axis=AX.X))
        self.i("vector", "s_pro", 1)
        self.w("vector", "s_pro", self.c["s_pro"])
        if self.on("vector"):
            self.em(self.e.tensor_scalar(out=t["lna"][:, 1:2], in0=t["lna"][:, 0:1],
                                         scalar1=1.0 / 1024.0, scalar2=None,
                                         op0=ALU.mult))
        self.i("vector", "s_pro", 1)
        self.w("vector", "s_pro", self.c["s_pro"])
        if self.on("vector"):
            v = self.e
            v.tensor_scalar(out=ebuf, in0=y, scalar1=t["lna"][:, 1:2],
                            scalar2=None, op0=ALU.subtract)
            v.tensor_mul(y, ebuf, ebuf)
            self.em(v.reduce_sum(t["lna"][:, 2:3], y, axis=AX.X))
        self.i("vector", "s_pro", 1)
        self.w("scalar", "s_pro", self.c["s_pro"])
        if self.on("scalar"):
            s = self.e
            s.activation(out=t["lna"][:, 3:4], in_=t["lna"][:, 2:3], func=AF.Ln,
                         bias=t["epsap"][:], scale=t["inv1024"][:])
            self.em(s.activation(out=t["lna"][:, 3:4], in_=t["lna"][:, 3:4],
                                 func=AF.Exp, bias=0.0, scale=t["mhalf"][:]))
        self.i("scalar", "s_pro", 1)
        self.w("vector", "s_pro", self.c["s_pro"])
        if self.on("vector"):
            self.em(self.e.tensor_scalar(out=y, in0=ebuf, scalar1=t["lna"][:, 3:4],
                                         scalar2=None, op0=ALU.mult))
        self.i("vector", sname, 1)

    def xf2_build(self, qb, ts):
        t = self.t
        self.w("vector", "s_ln", self.c["s_ln"])
        if self.on("vector"):
            self.em(self.e.tensor_copy(t["s1b"][:, 0:1024],
                                       t["s1"][:, ts * 1024:(ts + 1) * 1024]))
        self.i("vector", "s_pro", 1)
        spro = self.c["s_pro"]
        for g in range(2):
            def tposes(eng, half, g=g):
                if g == 0:
                    eng.wait_ge(self.sem["s_pro"], spro)
                ins = None
                for k in range(4):
                    ch = g * 4 + k
                    ins = eng.transpose(
                        t["psXF"][:, half * 512 + k * 128:half * 512 + (k + 1) * 128],
                        t["s1b"][:, ch * 128:(ch + 1) * 128], t["idb"][:])
                return ins

            def evac(eng, half, g=g):
                base = XF2SLOT[qb] * 4096
                for k in range(4):
                    ch = g * 4 + k
                    ins = eng.tensor_copy(
                        t["xf2"][:, base + ch * 512 + ts * 128:
                                 base + ch * 512 + (ts + 1) * 128],
                        t["psXF"][:, half * 512 + k * 128:
                                  half * 512 + (k + 1) * 128])
                return ins

            self.psxf_group(tposes, evac)

    def ffn(self, fq):
        t = self.t
        stage = self.cfg.get("ffn_stage", 5)
        self.w("vector", "m_out", self.c["m_out"])
        self.mix(fq, "r")
        if stage < 1:
            return
        for cch in range(8):
            self.ffn_mm_group("wr", cch, t["xrk"])
            self.w("scalar", "s_fp", self.c["s_fp"])
            if self.on("scalar"):
                sc = self.e
                ins = sc.activation(out=t["fstage"][:, 0:512],
                                    in_=t["psB7"][:], func=AF.Exp,
                                    bias=0.0, scale=t["m1"][:])
                self.em(ins)
            self.i("scalar", "s_fu", 1)
            if self.on("scalar"):
                sc = self.e
                sc.activation(out=t["fstage"][:, 512:1024], in_=t["fstage"][:, 0:512],
                              func=AF.Ln, bias=t["one1"][:], scale=1.0)
                self.em(sc.activation(out=t["rbuf"][:, cch * 512:(cch + 1) * 512],
                                      in_=t["fstage"][:, 512:1024],
                                      func=AF.Exp, bias=0.0, scale=t["m1"][:]))
            self.i("scalar", "s_pro", 1)
        if stage < 2:
            return
        self.mix(fq, "k")
        for cch in range(8):
            self.ffn_mm_group("wk", cch, t["xrk"])
            self.w("scalar", "s_fp", self.c["s_fp"])
            if self.on("scalar"):
                self.em(self.e.activation(out=t["t1st"][:], in_=t["psB7"][:],
                                          func=AF.Relu, bias=0.0, scale=1.0))
            self.i("scalar", "s_fu", 1)
            self.w("vector", "s_fu", self.c["s_fu"])
            if self.on("vector"):
                self.em(self.e.tensor_mul(t["k_act"][:, cch * 512:(cch + 1) * 512],
                                          t["t1st"][:], t["t1st"][:]))
            self.i("vector", "s_mix", 1)
        if stage < 3:
            return
        for dout in range(8):
            self.ffn_mm_group("wv", dout, t["k_act"])
            self.w("vector", "s_fp", self.c["s_fp"])
            self.w("vector", "s_pro", self.c["s_pro"])
            if self.on("vector"):
                self.em(self.e.tensor_mul(t["fstage"][:, 512:1024], t["psB7"][:],
                                          t["rbuf"][:, dout * 512:(dout + 1) * 512]))
            self.i("vector", "s_fu", 1)
            if stage < 4:
                continue
            for ts in range(4):
                a = self.c["s_ept2"]
                col = (a % 4) * 512 + ((a // 4) % 4) * 128
                self.w("tensor", "s_eps", a - 2)
                self.w("tensor", "s_fu", self.c["s_fu"])
                if dout == 0 and ts == 0:
                    self.w("tensor", "s_ex", self.c["s_ex"])
                if self.on("tensor"):
                    self.em(self.e.transpose(
                        t["psS"][:, col:col + 128],
                        t["fstage"][:, 512 + ts * 128:512 + (ts + 1) * 128],
                        t["idf"][:]))
                self.i("tensor", "s_ept2", 1)
                self.w("vector", "s_ept2", a + 1)
                if fq == 0 and ts == 0 and dout == 0:
                    self.w("vector", "m_sp", 32)
                if self.on("vector"):
                    self.em(self.e.tensor_add(
                        t["xin"][:, ts * 1024 + dout * 128:
                                 ts * 1024 + (dout + 1) * 128],
                        t["psS"][:, col:col + 128],
                        t["s1"][:, ts * 1024 + dout * 128:
                                ts * 1024 + (dout + 1) * 128]))
                self.i("vector", "s_eps", 1)
        if stage < 5:
            return
        for ts in range(4):
            self.ln(t["xin"], ts, first=False)
            if self.cfg.get("no_out"):
                continue
            oq = "sync"
            self.w(oq, "s_l2", self.c["s_l2"])
            if self.on(oq):
                row = fq * 512 + ts * 128
                self.em(self.e.dma_start(
                    out=t["dout"].ap()[row:row + 128, :],
                    in_=t["xin"][:, ts * 1024:(ts + 1) * 1024]))
            self.i(oq, "m_out", 16)

    def mix(self, fq, which):
        t = self.t
        mu = t["mu2"][:, 0:8] if which == "r" else t["mu2"][:, 16:24]
        mui = t["mu2"][:, 8:16] if which == "r" else t["mu2"][:, 24:32]
        xs = XF2SLOT[fq] * 4096
        ps = XF2SLOT[fq - 1 if fq >= 1 else 7] * 4096
        if which == "k":
            self.w("vector", "s_fp", self.c["s_fp"])
            self.w("vector", "s_pro", self.c["s_pro"])
        for ch in range(8):
            if self.on("vector"):
                v = self.e
                xsl = t["xf2"][:, xs + ch * 512:xs + (ch + 1) * 512]
                v.tensor_scalar(out=t["fstage"][:, 0:512], in0=xsl,
                                scalar1=mu[:, ch:ch + 1], scalar2=None,
                                op0=ALU.mult)
                if fq >= 1:
                    prev = t["xf2"][:, ps + ch * 512:ps + (ch + 1) * 512]
                    ins = v.scalar_tensor_tensor(
                        out=t["xrk"][:, ch * 512:(ch + 1) * 512], in0=prev,
                        scalar=mui[:, ch:ch + 1], in1=t["fstage"][:, 0:512],
                        op0=ALU.mult, op1=ALU.add)
                else:
                    prev = t["xf2"][:, ps + ch * 512:ps + ch * 512 + 511]
                    v.scalar_tensor_tensor(
                        out=t["xrk"][:, ch * 512 + 1:(ch + 1) * 512], in0=prev,
                        scalar=mui[:, ch:ch + 1], in1=t["fstage"][:, 1:512],
                        op0=ALU.mult, op1=ALU.add)
                    ins = v.tensor_copy(t["xrk"][:, ch * 512:ch * 512 + 1],
                                        t["fstage"][:, 0:1])
                self.em(ins)
            self.i("vector", "s_mix", 1)

    def ffn_mm_group(self, wname, g, rhs):
        t = self.t
        wi = {"wr": 0, "wk": 1, "wv": 2}[wname]
        seq = self.c["m_wrl"] // 16
        slot = seq % 5
        self.w("sync", "s_fp", seq - 4)
        self.w("sync", "m_wst", 16 * 24)
        if self.on("sync"):
            off = (wi * 8 + g) * (128 * 8 * 128)
            src = bass.AP(t["dwscr"], off, [[1024, 128], [1, 1024]])
            self.em(self.e.dma_start(
                out=t["wring"][:, slot * 1024:(slot + 1) * 1024], in_=src))
        self.i("sync", "m_wrl", 16)
        self.w("tensor", "m_wrl", 16 * (seq + 1))
        self.w("tensor", "s_fu", self.c["s_fu"])
        self.w("tensor", "s_epr", self.c["s_epr"])
        self.w("tensor", "s_mix", self.c["s_mix"])
        if self.on("tensor"):
            tn = self.e
            ins = None
            for dm in range(8):
                ins = tn.matmul(t["psB7"][:],
                                t["wring"][:, slot * 1024 + dm * 128:
                                           slot * 1024 + (dm + 1) * 128],
                                rhs[:, dm * 512:(dm + 1) * 512],
                                start=(dm == 0), stop=(dm == 7))
            self.em(ins)
        self.i("tensor", "s_fp", 1)

    def ffn_zero(self):
        t = self.t
        self.w("gpsimd", "s_eps", self.c["s_eps"])
        if self.on("gpsimd"):
            self.em(self.e.dma_start(out=t["s1"][:], in_=t["ds1scr"].ap()))
        self.i("gpsimd", "m_sp", 16)
        self.w("vector", "m_sp", 32)
        self.w("tensor", "m_sp", 32)
        self.ffn(0)

    def finale(self):
        oq = "sync"
        for name in self.cfg.get("dbg", []):
            for sn in ("s_eps", "s_ex", "s_ln", "s_l2", "s_xfe", "s_pve",
                       "s_epr", "s_mix", "s_pro", "s_zt"):
                self.w(oq, sn, self.c[sn])
            if self.on(oq):
                self.em(self.e.dma_start(out=self.t["dbg_" + name].ap(),
                                         in_=self.t[name][:]))
            self.i(oq, "m_out", 16)
        if self.on(oq) and self.c["m_out"] > 0:
            self.e.wait_ge(self.sem["m_out"], self.c["m_out"])


def _build_nc(cfg=None):
    nc = bass.Bass("TRN2", target_bir_lowering=False)
    t = {}
    t["dx"] = nc.dram_tensor("x", [T, D], F32, kind="ExternalInput")
    t["dwr"] = nc.dram_tensor("wrT", [D, D], F32, kind="ExternalInput")
    t["dwk"] = nc.dram_tensor("wkT", [D, D], F32, kind="ExternalInput")
    t["dwv"] = nc.dram_tensor("wvT", [D, D], F32, kind="ExternalInput")
    t["dmur"] = nc.dram_tensor("mur", [D], F32, kind="ExternalInput")
    t["dmuk"] = nc.dram_tensor("muk", [D], F32, kind="ExternalInput")
    t["dipw"] = nc.dram_tensor("ipw", [128, 1], F32, kind="ExternalInput")
    t["dout"] = nc.dram_tensor("out", [T, D], F32, kind="ExternalOutput")
    t["dwscr"] = nc.dram_tensor("wscr", [3 * 8 * 128 * 8 * 128], BF16,
                                kind="Internal")
    t["ds1scr"] = nc.dram_tensor("s1scr", [128, 4096], F32, kind="Internal")
    t["dcrow"] = nc.dram_tensor("crow", [128], F32, kind="Internal")

    es = ExitStack()
    sb = [
        ("xt", [128, NTB * XTB], BF16),
        ("xf", [128, 8192], BF16),
        ("xf2", [128, 16384], BF16),
        ("pun", [128, 3072], BF16),
        ("pvraw", [128, 4096], F32),
        ("s1", [128, 4096], F32),
        ("xin", [128, 4096], F32),
        ("rbuf", [128, 4096], FP16),
        ("k_act", [128, 4096], BF16),
        ("xrk", [128, 4096], BF16),
        ("wring", [128, 5120], BF16),
        ("fstage", [128, 1024], F32),
        ("t1st", [128, 512], FP16),
        ("recipz", [128, 64], F32),
        ("idb", [128, 128], BF16),
        ("idf", [128, 128], F32),
        ("mu2", [128, 32], F32),
        ("negc", [128, 1], F32),
        ("wsc", [128, 1], F32),
        ("scl_i", [128, 1], F32),
        ("m1", [128, 1], F32),
        ("epsap", [128, 1], F32),
        ("inv1024", [128, 1], F32),
        ("mhalf", [128, 1], F32),
        ("one1", [128, 1], F32),
        ("c128", [128, 1], F32),
        ("lna", [128, 4], F32),
        ("ipw1", [128, 1], F32),
        ("c11", [1, 4], F32),
    ]
    for name, shape, dt in sb:
        t[name] = es.enter_context(nc.sbuf_tensor(name, shape, dt))
    t["s1b"] = t["xrk"]
    t["zst2"] = t["fstage"]
    t["onesf"] = t["fstage"].ap()[0:1, 512:640]
    t["crow"] = t["fstage"].ap()[0:1, 640:768]
    t["m8row"] = t["fstage"].ap()[0:1, 768:896]
    for name, shape, dt in [
        ("psS", [128, 2048], F32),
        ("psPV0", [128, 512], F32),
        ("psPV1", [128, 512], F32),
        ("psXF", [128, 1024], BF16),
        ("psB7", [128, 512], F32),
    ]:
        t[name] = es.enter_context(nc.psum_tensor(name, shape, dt))

    if (cfg or {}).get("dump_y_qb", -1) >= 0:
        t["dbg_ydump"] = nc.dram_tensor("dbg_ydump", [512, 1024], F32,
                                        kind="ExternalOutput")
    for name in (cfg or {}).get("dbg", []):
        th = t[name]
        shape = list(th.shape) if hasattr(th, "shape") else None
        t["dbg_" + name] = nc.dram_tensor("dbg_" + name, shape, th.dtype,
                                          kind="ExternalOutput")
    sem = {}
    for name in SEMS:
        sem[name] = es.enter_context(nc.semaphore(name))

    with es, nc.Block() as block:
        def runner(role):
            def f(eng):
                with nc.allow_low_precision("bf16 matmul-operand staging"):
                    W(role, eng, t, sem, cfg).walk()
            return f

        block.sync(runner("sync"))
        block.gpsimd(runner("gpsimd"))
        block.tensor(runner("tensor"))
        block.scalar(runner("scalar"))
        block.vector(runner("vector"))
    return nc


_PI = None


def _perms():
    global _PI
    if _PI is None:
        tt = np.arange(T)
        pi = (tt % 8) * 512 + tt // 8
        inv = (tt % 512) * 8 + tt // 512
        _PI = (pi, inv)
    return _PI


def kernel(L0_attn_x, L0_attn_inter_palace_weight, L0_ffn_W_r, L0_ffn_W_k,
           L0_ffn_W_v, L0_ffn_mu_r, L0_ffn_mu_k):
    x = np.asarray(L0_attn_x, np.float32)
    ipw = np.asarray(L0_attn_inter_palace_weight, np.float32).reshape(1, 1)
    wrT = np.ascontiguousarray(np.asarray(L0_ffn_W_r, np.float32).T)
    wkT = np.ascontiguousarray(np.asarray(L0_ffn_W_k, np.float32).T)
    wvT = np.ascontiguousarray(np.asarray(L0_ffn_W_v, np.float32).T)
    mur = np.ascontiguousarray(np.asarray(L0_ffn_mu_r, np.float32))
    muk = np.ascontiguousarray(np.asarray(L0_ffn_mu_k, np.float32))

    pi, inv = _perms()
    nc = _build_nc()
    in_maps = []
    for b in range(NCORE):
        in_maps.append({
            "x": np.ascontiguousarray(x[b][inv]),
            "wrT": wrT, "wkT": wkT, "wvT": wvT,
            "mur": mur, "muk": muk,
            "ipw": np.full((128, 1), ipw[0, 0], np.float32),
        })
    res = run_bass_kernel_spmd(nc, in_maps, core_ids=list(range(NCORE)))
    out = np.empty((NCORE, T, D), np.float32)
    for b in range(NCORE):
        out[b] = np.asarray(res.results[b]["out"], np.float32)[pi]
    return out


if __name__ == "__main__":
    _build_nc()
    print("built ok")


# version 21
# speedup vs baseline: 1.1517x; 1.1517x over previous
import numpy as np
import ml_dtypes
from contextlib import ExitStack

import concourse.bass as bass
import concourse.mybir as mybir
from concourse.bass_utils import run_bass_kernel_spmd

F32 = mybir.dt.float32
BF16 = mybir.dt.bfloat16
FP16 = mybir.dt.float16
AF = mybir.ActivationFunctionType
ALU = mybir.AluOpType
AX = mybir.AxisListType

T, D, H, HD = 4096, 1024, 16, 64
NQB, NKB, NTB, NHP = 8, 32, 32, 8
XW = 65
XTB = H * XW
NCORE = 8

SEMS = [
    "m_xl", "m_il", "m_wld", "m_wst", "m_wrl", "m_xin", "m_out", "m_sp", "m_wpl",
    "s_prep", "s_xc", "s_pro", "s_wpc",
    "s_sc", "s_ex", "s_pc", "s_pv", "s_pve",
    "s_xft", "s_xfe", "s_zt", "s_epr", "s_ept2", "s_eps", "s_ln",
    "s_mix", "s_fp", "s_fu", "s_wc", "s_ft", "s_fz", "s_l2", "s_lnc",
]

XF2SLOT = {0: 0, 7: 3}
for _q in range(1, 7):
    XF2SLOT[_q] = 1 + (_q - 1) % 2


class W:

    def __init__(self, role, eng, tn, sem, cfg=None):
        self.role = role
        self.e = eng
        self.t = tn
        self.sem = sem
        self.cfg = cfg or {}
        self.c = {name: 0 for name in SEMS}
        self.grp_fp = []
        self.rz_done = 0
        self.wprep_j = 0
        self.pending_z = None
        self.last = None
        self.xfe_done = {}
        self.pro_end = 0

    def on(self, role):
        return self.role == role

    def w(self, role, sname, val):
        if self.role == role and val > 0:
            self.e.wait_ge(self.sem[sname], val)

    def i(self, role, sname, n=1):
        self.c[sname] += n
        if self.role == role:
            assert self.last is not None, f"inc {sname} with no instruction"
            self.last.then_inc(self.sem[sname], n)

    def em(self, ins):
        self.last = ins
        return ins

    def walk(self):
        n_hp = self.cfg.get("n_hp", NQB * NHP)
        do_ep = self.cfg.get("do_ep", True)
        do_ffn = self.cfg.get("do_ffn", True)
        self.fq = []
        self.prologue()
        if n_hp <= 0:
            self.prologue2()
            self.pro_end = self.c["s_pro"]
            self.finale()
            return
        self.build_xf_serial(0, 0)
        self.prologue2()
        self.pro_end = self.c["s_pro"]
        done = 0
        complete = True
        for qb in range(NQB):
            for hp in range(NHP):
                if done >= n_hp:
                    complete = False
                    break
                self.hp_phase(qb, hp)
                done += 1
            if not complete:
                break
            if do_ep:
                self.pump_all()
                self.epilogue(qb)
                if do_ffn and qb >= 1:
                    self.ffn_enqueue(qb)
        self.pump_all()
        if complete and do_ep and do_ffn:
            self.ffn_zero()
        self.finale()

    def pump(self, n=1):
        for _ in range(n):
            if not self.fq:
                return
            step = self.fq.pop(0)
            step()

    def pump_all(self):
        while self.fq:
            self.fq.pop(0)()

    def prologue(self):
        t, c = self.t, self.c
        if self.on("gpsimd"):
            g = self.e
            for idt, n in ((t["idb"], 128), (t["idf"], 128)):
                g.memset(idt[:], 0.0)
                g.affine_select(out=idt[:], in_=idt[:], compare_op=ALU.not_equal,
                                fill=1.0, base=0, pattern=[[-1, n]],
                                channel_multiplier=1)
            g.memset(t["scl_i"][:], 0.125)
            g.memset(t["m1"][:], -1.0)
            g.memset(t["epsap"][:], 1e-5)
            g.memset(t["inv1024"][:], 1.0 / 1024.0)
            g.memset(t["mhalf"][:], -0.5)
            g.memset(t["one1"][:], 1.0)
            g.memset(t["onesf"], 1.0)
            g.memset(t["m8row"], -0.125)
            xtones = t["xt"].ap().rearrange("p (tb h w) -> p tb h w", h=H, w=XW)
            self.em(g.memset(xtones[:, :, :, 64:65], 1.0))
        self.i("gpsimd", "s_prep", 1)

        xst = t["pvraw"]
        for tb in range(NTB):
            sl = tb % 4
            q = "sync" if tb % 2 == 0 else "gpsimd"
            self.w(q, "s_xc", tb - 3)
            if self.on(q):
                self.em(self.e.dma_start(out=xst[:, sl * 1024:(sl + 1) * 1024],
                                         in_=t["dx"].ap()[tb * 128:(tb + 1) * 128, :]))
            self.i(q, "m_xl" if q == "sync" else "m_wld", 16)
            self.w("vector", "m_xl", 16 * (tb // 2 + 1))
            self.w("vector", "m_wld", 16 * ((tb + 1) // 2))
            self.w("vector", "s_prep", 1)
            if self.on("vector"):
                dst = t["xt"].ap().rearrange("p (tb h w) -> p tb h w", h=H, w=XW)
                src = xst[:, sl * 1024:(sl + 1) * 1024].rearrange(
                    "p (h d) -> p h d", d=HD)
                self.em(self.e.tensor_copy(dst[:, tb, :, 0:64], src))
            self.i("vector", "s_xc", 1)

        self.prologue2_done = False
        return

    def prologue2(self):
        t, c = self.t, self.c
        n2 = t["s1"][:, 0:512]
        sq = t["xin"][:, 0:1024]
        if self.on("vector"):
            v = self.e
            xtv = t["xt"].ap().rearrange("p (tb h w) -> p tb h w", h=H, w=XW)
            sq3 = sq.rearrange("p (h d) -> p h d", d=HD)
            for tb in range(NTB):
                vsl = xtv[:, tb, :, 0:64]
                v.tensor_mul(sq3, vsl, vsl)
                ins = v.reduce_sum(n2[:, tb * 16:(tb + 1) * 16], sq3, axis=AX.X)
            self.em(ins)
        self.i("vector", "s_pro", 1)
        self.w("vector", "s_pro", c["s_pro"])
        if self.on("vector"):
            self.em(self.e.reduce_max(t["c128"][:], n2, axis=AX.X))
        self.i("vector", "s_pro", 1)
        self.w("sync", "s_pro", c["s_pro"])
        if self.on("sync"):
            self.em(self.e.dma_start(out=t["dcrow"].ap(), in_=t["c128"][:]))
        self.i("sync", "m_il", 16)
        self.w("sync", "m_il", c["m_il"])
        if self.on("sync"):
            self.em(self.e.dma_start(out=t["crow"], in_=t["dcrow"].ap()))
        self.i("sync", "m_il", 16)
        if self.on("sync"):
            self.em(self.e.dma_start(out=t["ipw1"][:], in_=t["dipw"].ap()))
        self.i("sync", "m_il", 16)
        for dname, base in (("dmur", 0), ("dmuk", 16)):
            for ch in range(8):
                if self.on("sync"):
                    src = self.t[dname].ap().rearrange("(c p) -> c p", p=128)
                    self.em(self.e.dma_start(out=t["mu2"][:, base + ch:base + ch + 1],
                                             in_=src[ch:ch + 1, :]))
                self.i("sync", "m_il", 16)
        self.w("vector", "m_il", c["m_il"])
        if self.on("vector"):
            self.em(self.e.reduce_max(t["c11"][0:1, 0:1], t["crow"], axis=AX.X))
        self.i("vector", "s_pro", 1)
        if self.on("scalar"):
            from concourse.hw_specs import get_activation_tables
            names = list(get_activation_tables(self.e.bass.m.arch).keys())
            idx = names.index("natural_log_exp_and_others")
            nc0 = self.e.bass
            self.e.add_instruction(mybir.InstLoadActFuncSet(
                name=nc0.get_next_instruction_name(),
                act_func_set_id=idx, ins=[], outs=[]))
        self.w("scalar", "m_il", c["m_il"])
        self.w("scalar", "s_prep", 1)
        if self.on("scalar"):
            self.em(self.e.activation(out=t["wsc"][:], in_=t["ipw1"][:],
                                      func=AF.Exp, bias=0.0, scale=t["m1"][:]))
        self.i("scalar", "s_pro", 1)
        self.w("vector", "s_pro", c["s_pro"])
        if self.on("vector"):
            self.em(self.e.tensor_copy(t["lna"][:, 0:1], t["wsc"][:]))
        self.i("vector", "s_pro", 1)
        self.w("scalar", "s_pro", c["s_pro"])
        if self.on("scalar"):
            sc = self.e
            sc.activation(out=t["lna"][:, 1:2], in_=t["lna"][:, 0:1], func=AF.Ln,
                          bias=t["one1"][:], scale=1.0)
            self.em(sc.activation(out=t["wsc"][:], in_=t["lna"][:, 1:2],
                                  func=AF.Exp, bias=0.0, scale=t["m1"][:]))
        self.i("scalar", "s_pro", 1)
        self.w("vector", "s_pro", c["s_pro"])
        if self.on("vector"):
            self.em(self.e.tensor_scalar(out=t["wsc"][:], in0=t["wsc"][:],
                                         scalar1=0.125, scalar2=None,
                                         op0=ALU.mult))
        self.i("vector", "s_pro", 1)
        self.w("tensor", "s_pro", c["s_pro"])
        self.w("tensor", "s_prep", 1)
        if self.on("tensor"):
            self.em(self.e.matmul(t["psB7"][:, 128:129], t["m8row"],
                                  t["c11"][0:1, 0:1], start=True, stop=True))
        self.i("tensor", "s_pro", 1)
        self.w("vector", "s_pro", c["s_pro"])
        if self.on("vector"):
            v = self.e
            v.tensor_copy(t["negc"][:], t["psB7"][:, 128:129])
            v.tensor_scalar(out=t["mu2"][:, 8:16], in0=t["mu2"][:, 0:8],
                            scalar1=-1.0, scalar2=1.0, op0=ALU.mult, op1=ALU.add)
            self.em(v.tensor_scalar(out=t["mu2"][:, 24:32], in0=t["mu2"][:, 16:24],
                                    scalar1=-1.0, scalar2=1.0, op0=ALU.mult,
                                    op1=ALU.add))
        self.i("vector", "s_pro", 1)



    def wprep_step(self):
        t = self.t
        j = self.wprep_j
        if j >= 24:
            return
        self.wprep_j += 1
        srcs = [("dwr", 0), ("dwk", 1), ("dwv", 2)]
        dname, wi = srcs[j // 8]
        dm = j % 8
        sl = j % 2
        wst = t["pvraw"]
        wbf = t["wring"]
        if j < 2:
            self.w("gpsimd", "s_xc", NTB)
        self.w("gpsimd", "s_wpc", j - 1)
        if self.on("gpsimd"):
            self.em(self.e.dma_start(
                out=wst[:, 2048 + sl * 1024:2048 + (sl + 1) * 1024],
                in_=t[dname].ap()[dm * 128:(dm + 1) * 128, :]))
        self.i("gpsimd", "m_wpl", 16)
        self.w("gpsimd", "m_wpl", 16 * (j + 1))
        self.w("gpsimd", "m_wst", 16 * (j - 1))
        if self.on("gpsimd"):
            self.em(self.e.tensor_copy(
                wbf[:, sl * 1024:(sl + 1) * 1024],
                wst[:, 2048 + sl * 1024:2048 + (sl + 1) * 1024]))
        self.i("gpsimd", "s_wpc", 1)
        self.w("gpsimd", "s_wpc", j + 1)
        if self.on("gpsimd"):
            off = wi * (8 * 128 * 8 * 128) + dm * 128
            dst = bass.AP(t["dwscr"], off,
                          [[8 * 128, 128], [128 * 8 * 128, 8], [1, 128]])
            self.em(self.e.dma_start(
                out=dst, in_=wbf[:, sl * 1024:(sl + 1) * 1024]))
        self.i("gpsimd", "m_wst", 16)

    def psxf_group(self, tposes, evac):
        g = self.c["s_xft"]
        half = g % 2
        self.w("tensor", "s_xfe", g)
        if self.on("tensor"):
            self.em(tposes(self.e, half))
        self.i("tensor", "s_xft", 1)
        self.w("vector", "s_xft", g + 1)
        if self.on("vector"):
            self.em(evac(self.e, half))
        self.i("vector", "s_xfe", 1)

    def _xf_tposes_evac(self, B, hp, g):
        t = self.t
        sl = B % 2
        h0, h1 = 2 * hp, 2 * hp + 1

        def tposes(eng, half):
            ins = None
            for k in range(4):
                tb = g * 4 + k
                eng.transpose(t["psXF"][0:64, half * 512 + k * 128:
                                        half * 512 + (k + 1) * 128],
                              t["xt"][:, tb * XTB + h0 * XW:tb * XTB + h0 * XW + 64],
                              t["idb"][:])
                ins = eng.transpose(t["psXF"][64:128, half * 512 + k * 128:
                                              half * 512 + (k + 1) * 128],
                                    t["xt"][:, tb * XTB + h1 * XW:
                                            tb * XTB + h1 * XW + 64],
                                    t["idb"][:], tile_position=(0, 64))
            return ins

        def evac(eng, half):
            return eng.tensor_copy(
                t["xf"][:, sl * 4096 + g * 512:sl * 4096 + (g + 1) * 512],
                t["psXF"][:, half * 512:(half + 1) * 512])

        return tposes, evac

    def build_xf_serial(self, qb, hp):
        B = qb * NHP + hp
        self.w("tensor", "s_xc", NTB)
        self.w("tensor", "s_prep", 1)
        self.w("vector", "s_sc", (B - 1) * NKB)
        for g in range(8):
            tposes, evac = self._xf_tposes_evac(B, hp, g)
            self.psxf_group(tposes, evac)
        self.xfe_done[B] = self.c["s_xfe"]

    def build_xf_group(self, qb, hp, g):
        B = qb * NHP + hp
        if g == 0:
            self.w("vector", "s_sc", (B - 1) * NKB)
        tposes, evac = self._xf_tposes_evac(B, hp, g)
        self.psxf_group(tposes, evac)
        if g == 7:
            self.xfe_done[B] = self.c["s_xfe"]

    def hp_phase(self, qb, hp):
        t = self.t
        B = qb * NHP + hp
        sl = B % 2
        xfb = sl * 4096
        n_kb = self.cfg.get("n_kb", NKB)

        nxt = None
        if B + 1 < NQB * NHP:
            nxt = divmod(B + 1, NHP)

        for kb in range(n_kb):
            g = B * NKB + kb
            p = kb % 2
            self.w("tensor", "s_ex", g - 1)
            if kb == 0:
                self.w("tensor", "s_xfe", self.xfe_done[B])
                self.w("tensor", "s_eps", self.c["s_eps"])
            if self.on("tensor"):
                tn = self.e
                tn.matmul(t["psS"][:, p * 1024:p * 1024 + 512],
                          t["xf"][0:64, xfb + kb * 128:xfb + (kb + 1) * 128],
                          t["xf"][0:64, xfb + qb * 512:xfb + (qb + 1) * 512],
                          start=True, stop=True, tile_position=(0, 0))
                self.em(tn.matmul(t["psS"][:, p * 1024 + 512:(p + 1) * 1024],
                                  t["xf"][64:128, xfb + kb * 128:xfb + (kb + 1) * 128],
                                  t["xf"][64:128, xfb + qb * 512:xfb + (qb + 1) * 512],
                                  start=True, stop=True, tile_position=(64, 0)))
            self.i("tensor", "s_sc", 1)

            if kb == 1:
                self.z_chain()
            if kb >= 1:
                self.pv_pair(qb, hp, kb - 1)

            self.w("scalar", "s_sc", g + 1)
            self.w("scalar", "s_pc", g - 2)
            if g == 0:
                self.w("scalar", "s_pro", self.pro_end)
            intra = (kb // 4) == qb
            if self.on("scalar"):
                scale = t["scl_i"][:] if intra else t["wsc"][:]
                self.em(self.e.activation(
                    out=t["pun"][:, (g % 3) * 1024:(g % 3 + 1) * 1024],
                    in_=t["psS"][:, p * 1024:(p + 1) * 1024],
                    func=AF.Exp, bias=t["negc"][:], scale=scale))
            self.i("scalar", "s_ex", 1)

            if nxt is not None and 2 <= kb < 18 and (kb - 2) % 2 == 0:
                self.build_xf_group(nxt[0], nxt[1], (kb - 2) // 2)
            if qb == 0 and hp < 3 and 19 <= kb < 27:
                self.wprep_step()
            if kb % 4 == 3:
                self.pump()

        if n_kb < NKB:
            return
        self.pv_pair(qb, hp, NKB - 1)
        if self.cfg.get("no_evac"):
            return

        self.w("vector", "s_pc", (B + 1) * NKB)
        self.w("vector", "s_fu", self.c["s_fu"])
        self.w("vector", "s_pro", self.c["s_pro"])
        if qb == 0 and hp == 4:
            self.w("vector", "m_wpl", 16 * 24)
        emask = self.cfg.get("evac_mask", 15)
        if self.on("vector"):
            v = self.e
            ins = v.engine_nop()
            if emask & 1:
                ins = v.tensor_copy(t["pvraw"][0:64, hp * 512:(hp + 1) * 512],
                                    t["psPV0"][0:64, :])
            if emask & 2:
                ins = v.tensor_copy(t["pvraw"][64:128, hp * 512:(hp + 1) * 512],
                                    t["psPV1"][0:64, :])
            if emask & 4:
                ins = v.tensor_copy(t["zst2"][0:1, 0:512], t["psPV0"][64:65, :])
            if emask & 8:
                ins = v.tensor_copy(t["zst2"][0:1, 512:1024], t["psPV1"][64:65, :])
            self.em(ins)
        self.i("vector", "s_pve", 1)
        if self.cfg.get("no_ztr"):
            return

        self.pending_z = (qb, hp, B)

    def z_chain(self):
        if self.pending_z is None:
            return
        qb, hp, B = self.pending_z
        self.pending_z = None
        t = self.t
        self.w("tensor", "s_pve", B + 1)
        self.w("tensor", "s_fu", self.c["s_fu"])
        self.w("tensor", "s_eps", self.c["s_eps"])
        if self.on("tensor"):
            tn = self.e
            ins = None
            for half, base in ((0, 0), (1, 512)):
                for ts in range(4):
                    ins = tn.transpose(
                        t["psB7"][:, ts * 2 + half:ts * 2 + half + 1],
                        t["zst2"][0:1, base + ts * 128:base + (ts + 1) * 128],
                        t["idf"][0:1, 0:1])
            ins = tn.matmul(t["psB7"][:, 8:9], t["zst2"][0:1, 0:128],
                            t["zst2"][0:1, 0:1], start=True, stop=True)
            self.em(ins)
        self.i("tensor", "s_fp", 1)
        self.w("scalar", "s_fp", self.c["s_fp"])
        if self.on("scalar"):
            sc = self.e
            sc.activation(out=t["lnz"][:], in_=t["psB7"][:, 0:8],
                          func=AF.Ln, bias=0.0, scale=1.0)
            self.em(sc.activation(out=t["recipz"][:, hp * 8:(hp + 1) * 8],
                                  in_=t["lnz"][:], func=AF.Exp, bias=0.0,
                                  scale=t["m1"][:]))
        self.i("scalar", "s_fu", 1)
        if hp == NHP - 1:
            self.rz_done = self.c["s_fu"]

    def pv_pair(self, qb, hp, kb):
        t = self.t
        B = qb * NHP + hp
        g = B * NKB + kb
        h0, h1 = 2 * hp, 2 * hp + 1
        self.w("tensor", "s_ex", g + 1)
        if kb == 0:
            self.w("tensor", "s_pve", B)
        if self.on("tensor"):
            tn = self.e
            start, stop = kb == 0, kb == NKB - 1
            tn.matmul(t["psPV0"][0:65, :],
                      t["xt"][:, kb * XTB + h0 * XW:kb * XTB + (h0 + 1) * XW],
                      t["pun"][:, (g % 3) * 1024:(g % 3) * 1024 + 512],
                      start=start, stop=stop)
            self.em(tn.matmul(t["psPV1"][0:65, :],
                              t["xt"][:, kb * XTB + h1 * XW:kb * XTB + (h1 + 1) * XW],
                              t["pun"][:, (g % 3) * 1024 + 512:(g % 3 + 1) * 1024],
                              start=start, stop=stop))
        self.i("tensor", "s_pc", 1)

    def epilogue(self, qb):
        t = self.t
        eps = self.cfg.get("ep_stage", 9)
        self.z_chain()
        self.w("vector", "m_out", self.c["m_out"])
        self.w("vector", "s_l2", self.c["s_l2"])
        self.w("vector", "s_lnc", self.c["s_lnc"])
        if eps < 1:
            return
        xdst = t["pvraw"] if self.cfg.get("xin_to_pvraw") else t["xin"]
        nxin = self.cfg.get("n_xin", 4)
        self.w("sync", "s_fu", self.c["s_fu"])
        self.w("sync", "s_l2", self.c["s_l2"])
        self.w("sync", "s_lnc", self.c["s_lnc"])
        self.w("sync", "s_ln", self.c["s_ln"])
        self.w("sync", "m_out", self.c["m_out"])
        if qb == 1:
            self.w("sync", "m_sp", 16)
        for ts in range(4):
            if ts >= nxin:
                continue
            if self.on("sync"):
                row = qb * 512 + ts * 128
                self.em(self.e.dma_start(
                    out=xdst[:, ts * 1024:(ts + 1) * 1024],
                    in_=t["dx"].ap()[row:row + 128, :]))
            self.i("sync", "m_xin", 16)

        if eps < 2:
            return
        for ts in range(4):
            for hp in range(NHP):
                a = self.c["s_ept2"]
                col = (a % 4) * 512 + ((a // 4) % 4) * 128
                self.w("tensor", "s_eps", a - 2)
                if ts == 0 and hp == 0:
                    self.w("tensor", "s_ex", (qb + 1) * NHP * NKB)
                    self.w("tensor", "s_pve", (qb + 1) * NHP)
                if self.on("tensor"):
                    self.em(self.e.transpose(
                        t["psS"][:, col:col + 128],
                        t["pvraw"][:, hp * 512 + ts * 128:hp * 512 + (ts + 1) * 128],
                        t["idf"][:]))
                self.i("tensor", "s_ept2", 1)
                self.w("vector", "s_ept2", a + 1)
                if ts == 0 and hp == 0:
                    self.w("vector", "s_fu", self.rz_done)
                    self.w("vector", "m_xin", self.c["m_xin"])
                if self.on("vector"):
                    v = self.e
                    h0, h1 = 2 * hp, 2 * hp + 1
                    base = ts * 1024 + hp * 128
                    v.scalar_tensor_tensor(
                        out=t["s1"][:, base:base + 64],
                        in0=t["psS"][:, col:col + 64],
                        scalar=t["recipz"][:, hp * 8 + ts * 2:hp * 8 + ts * 2 + 1],
                        in1=t["xin"][:, base:base + 64],
                        op0=ALU.mult, op1=ALU.add)
                    self.em(v.scalar_tensor_tensor(
                        out=t["s1"][:, base + 64:base + 128],
                        in0=t["psS"][:, col + 64:col + 128],
                        scalar=t["recipz"][:, hp * 8 + ts * 2 + 1:hp * 8 + ts * 2 + 2],
                        in1=t["xin"][:, base + 64:base + 128],
                        op0=ALU.mult, op1=ALU.add))
                self.i("vector", "s_eps", 1)
            if qb == self.cfg.get("dump_y_qb", -1):
                self.w("sync", "s_eps", self.c["s_eps"])
                if self.on("sync"):
                    self.em(self.e.dma_start(
                        out=bass.AP(self.t["dbg_ydump"], ts * 128 * 1024,
                                    [[1024, 128], [1, 1024]]),
                        in_=t["s1"][:, ts * 1024:(ts + 1) * 1024]))
                self.i("sync", "m_sp", 16)
                self.w("vector", "m_sp", self.c["m_sp"])
            if eps < 3:
                continue
            self.w("scalar", "s_eps", self.c["s_eps"])
            if self.on("scalar"):
                self.em(self.e.activation(
                    out=t["xin"][:, ts * 1024:(ts + 1) * 1024],
                    in_=t["s1"][:, ts * 1024:(ts + 1) * 1024],
                    func=AF.Square, bias=0.0, scale=1.0,
                    accum_out=t["lnb"][:, ts:ts + 1]))
            self.i("scalar", "s_ln", 1)
            if self.on("scalar"):
                self.em(self.e.activation(
                    out=t["xin"][:, ts * 1024:(ts + 1) * 1024],
                    in_=t["s1"][:, ts * 1024:(ts + 1) * 1024],
                    func=AF.Copy, bias=0.0, scale=1.0,
                    accum_out=t["lnb"][:, 4 + ts:5 + ts]))
            self.i("scalar", "s_lnc", 1)

        if eps < 3:
            return
        self.w("scalar", "s_lnc", self.c["s_lnc"])
        if self.on("scalar"):
            sc = self.e
            L = t["lnb"]
            self.em(sc.activation(out=L[:, 8:12], in_=L[:, 4:8], func=AF.Copy,
                                  bias=0.0, scale=1.0 / 1024.0))
        self.i("scalar", "s_lnc", 1)
        self.w("scalar", "s_lnc", self.c["s_lnc"])
        if self.on("scalar"):
            sc = self.e
            L = t["lnb"]
            self.em(sc.activation(out=L[:, 12:16], in_=L[:, 8:12],
                                  func=AF.Square, bias=0.0,
                                  scale=1.0))
        self.i("scalar", "s_lnc", 1)
        self.w("scalar", "s_lnc", self.c["s_lnc"])
        if self.on("scalar"):
            L = t["lnb"]
            self.em(self.e.activation(out=L[:, 16:20], in_=L[:, 12:16],
                                      func=AF.Copy, bias=1e-5,
                                      scale=-1.0))
        self.i("scalar", "s_lnc", 1)
        self.w("scalar", "s_lnc", self.c["s_lnc"])
        if self.on("scalar"):
            sc = self.e
            L = t["lnb"]
            ins = None
            for ts2 in range(4):
                ins = sc.activation(out=L[:, 20 + ts2:21 + ts2],
                                    in_=L[:, ts2:ts2 + 1], func=AF.Ln,
                                    bias=L[:, 16 + ts2:17 + ts2],
                                    scale=t["inv1024"][:])
            self.em(ins)
        self.i("scalar", "s_lnc", 1)
        self.w("scalar", "s_lnc", self.c["s_lnc"])
        if self.on("scalar"):
            self.em(self.e.activation(out=t["lnb"][:, 28:32],
                                      in_=t["lnb"][:, 20:24],
                                      func=AF.Exp, bias=0.0,
                                      scale=t["mhalf"][:]))
        self.i("scalar", "s_lnc", 1)
        for ts in range(min(4, nxin)):
            self.w("vector", "s_lnc", self.c["s_lnc"])
            if self.on("vector"):
                self.em(self.e.tensor_scalar(
                    out=t["xin"][:, ts * 1024:(ts + 1) * 1024],
                    in0=t["s1"][:, ts * 1024:(ts + 1) * 1024],
                    scalar1=t["lnb"][:, 8 + ts:9 + ts],
                    scalar2=t["lnb"][:, 28 + ts:29 + ts],
                    op0=ALU.subtract, op1=ALU.mult))
            self.i("vector", "s_ln", 1)
            if eps >= 4:
                self.xf2_build(qb, ts)

        if qb == 0 and eps >= 5:
            self.w("gpsimd", "s_ln", self.c["s_ln"])
            if self.on("gpsimd"):
                self.em(self.e.dma_start(out=t["ds1scr"].ap(), in_=t["xin"][:]))
            self.i("gpsimd", "m_sp", 16)

    def ln(self, buf, ts, first, ebuf_t=None):
        t = self.t
        sname = "s_ln" if first else "s_l2"
        y = buf[:, ts * 1024:(ts + 1) * 1024]
        eb = ebuf_t if ebuf_t is not None else t["s1"]
        ebuf = eb[:, ts * 1024:(ts + 1) * 1024]
        if self.on("vector"):
            self.em(self.e.reduce_sum(t["lna"][:, 0:1], y, axis=AX.X))
        self.i("vector", "s_pro", 1)
        self.w("vector", "s_pro", self.c["s_pro"])
        if self.on("vector"):
            self.em(self.e.tensor_scalar(out=t["lna"][:, 1:2], in0=t["lna"][:, 0:1],
                                         scalar1=1.0 / 1024.0, scalar2=None,
                                         op0=ALU.mult))
        self.i("vector", "s_pro", 1)
        self.w("vector", "s_pro", self.c["s_pro"])
        if self.on("vector"):
            v = self.e
            v.tensor_scalar(out=ebuf, in0=y, scalar1=t["lna"][:, 1:2],
                            scalar2=None, op0=ALU.subtract)
            v.tensor_mul(y, ebuf, ebuf)
            self.em(v.reduce_sum(t["lna"][:, 2:3], y, axis=AX.X))
        self.i("vector", "s_pro", 1)
        self.w("scalar", "s_pro", self.c["s_pro"])
        if self.on("scalar"):
            self.em(self.e.activation(out=t["lna"][:, 3:4], in_=t["lna"][:, 2:3],
                                      func=AF.Ln, bias=t["epsap"][:],
                                      scale=t["inv1024"][:]))
        self.i("scalar", "s_pro", 1)
        self.w("scalar", "s_pro", self.c["s_pro"])
        if self.on("scalar"):
            self.em(self.e.activation(out=t["lna"][:, 2:3], in_=t["lna"][:, 3:4],
                                      func=AF.Exp, bias=0.0, scale=t["mhalf"][:]))
        self.i("scalar", "s_pro", 1)
        self.w("vector", "s_pro", self.c["s_pro"])
        if self.on("vector"):
            self.em(self.e.tensor_scalar(out=y, in0=ebuf, scalar1=t["lna"][:, 2:3],
                                         scalar2=None, op0=ALU.mult))
        self.i("vector", sname, 1)

    def xf2_build(self, qb, ts):
        t = self.t
        self.w("vector", "s_ln", self.c["s_ln"])
        if self.on("vector"):
            self.em(self.e.tensor_copy(t["s1b"][:, 0:1024],
                                       t["xin"][:, ts * 1024:(ts + 1) * 1024]))
        self.i("vector", "s_ln", 1)
        spro = self.c["s_ln"]
        for g in range(2):
            def tposes(eng, half, g=g):
                if g == 0:
                    eng.wait_ge(self.sem["s_ln"], spro)
                ins = None
                for k in range(4):
                    ch = g * 4 + k
                    ins = eng.transpose(
                        t["psXF"][:, half * 512 + k * 128:half * 512 + (k + 1) * 128],
                        t["s1b"][:, ch * 128:(ch + 1) * 128], t["idb"][:])
                return ins

            def evac(eng, half, g=g):
                base = XF2SLOT[qb] * 4096
                for k in range(4):
                    ch = g * 4 + k
                    ins = eng.tensor_copy(
                        t["xf2"][:, base + ch * 512 + ts * 128:
                                 base + ch * 512 + (ts + 1) * 128],
                        t["psXF"][:, half * 512 + k * 128:
                                  half * 512 + (k + 1) * 128])
                return ins

            self.psxf_group(tposes, evac)

    def ffn_enqueue(self, fq):
        t = self.t
        q = self.fq
        for ch in range(8):
            q.append(lambda ch=ch: self.mix_step(fq, "r", ch))
        for cch in range(8):
            q.append(lambda cch=cch: self.r_step(fq, cch))
        for ch in range(8):
            q.append(lambda ch=ch: self.mix_step(fq, "k", ch))
        for cch in range(8):
            q.append(lambda cch=cch: self.k_step(fq, cch))
        for dout in range(8):
            q.append(lambda dout=dout: self.v_step(fq, dout))
        for ts in range(4):
            q.append(lambda ts=ts: self.ln2_stat(fq, ts))
        q.append(lambda: self.ln2_chain(fq))
        for ts in range(4):
            q.append(lambda ts=ts: self.ln2_apply(fq, ts))

    def r_step(self, fq, cch):
        t = self.t
        self.ffn_mm_group("wr", cch, t["xrk"])
        self.w("scalar", "s_fp", self.c["s_fp"])
        if self.on("scalar"):
            self.em(self.e.activation(out=t["fstage"][:, 0:512],
                                      in_=t["psB7"][:], func=AF.Exp,
                                      bias=0.0, scale=t["m1"][:]))
        self.i("scalar", "s_fu", 1)
        if self.on("scalar"):
            sc = self.e
            sc.activation(out=t["fstage"][:, 512:1024], in_=t["fstage"][:, 0:512],
                          func=AF.Ln, bias=t["one1"][:], scale=1.0)
            self.em(sc.activation(out=t["rbuf"][:, cch * 512:(cch + 1) * 512],
                                  in_=t["fstage"][:, 512:1024],
                                  func=AF.Exp, bias=0.0, scale=t["m1"][:]))
        self.i("scalar", "s_pro", 1)

    def k_step(self, fq, cch):
        t = self.t
        self.ffn_mm_group("wk", cch, t["xrk"])
        self.w("scalar", "s_fp", self.c["s_fp"])
        if self.on("scalar"):
            self.em(self.e.activation(out=t["t1st"][:], in_=t["psB7"][:],
                                      func=AF.Relu, bias=0.0, scale=1.0))
        self.i("scalar", "s_fu", 1)
        self.w("vector", "s_fu", self.c["s_fu"])
        if self.on("vector"):
            self.em(self.e.tensor_mul(t["k_act"][:, cch * 512:(cch + 1) * 512],
                                      t["t1st"][:], t["t1st"][:]))
        self.i("vector", "s_mix", 1)

    def v_step(self, fq, dout):
        t = self.t
        self.ffn_mm_group("wv", dout, t["k_act"])
        self.w("vector", "s_fp", self.c["s_fp"])
        self.w("vector", "s_pro", self.c["s_pro"])
        if dout == 0:
            self.w("vector", "m_out", self.c["m_out"])
            self.w("vector", "s_ln", self.c["s_ln"])
            self.w("vector", "s_l2", self.c["s_l2"])
            self.w("vector", "s_lnc", self.c["s_lnc"])
        if self.on("vector"):
            self.em(self.e.tensor_mul(t["fstage"][:, 512:1024], t["psB7"][:],
                                      t["rbuf"][:, dout * 512:(dout + 1) * 512]))
        self.i("vector", "s_fu", 1)
        self.w("tensor", "s_fu", self.c["s_fu"])
        if self.on("tensor"):
            ins = None
            for ts in range(4):
                ins = self.e.transpose(
                    t["psB7"][:, ts * 128:(ts + 1) * 128],
                    t["fstage"][:, 512 + ts * 128:512 + (ts + 1) * 128],
                    t["idf"][:])
            self.em(ins)
        self.i("tensor", "s_fp", 1)
        res_src = t["s1"] if fq == 0 else t["xin"]
        z2_dst = t["xin"] if fq == 0 else t["s1"]
        self.w("vector", "s_fp", self.c["s_fp"])
        if self.on("vector"):
            ins = None
            for ts in range(4):
                ins = self.e.tensor_add(
                    z2_dst[:, ts * 1024 + dout * 128:
                           ts * 1024 + (dout + 1) * 128],
                    t["psB7"][:, ts * 128:(ts + 1) * 128],
                    res_src[:, ts * 1024 + dout * 128:
                            ts * 1024 + (dout + 1) * 128])
            self.em(ins)
        self.i("vector", "s_fu", 1)

    def _ln2_bufs(self, fq):
        t = self.t
        return (t["xin"], t["s1"]) if fq == 0 else (t["s1"], t["xin"])

    def ln2_stat(self, fq, ts):
        t = self.t
        buf, obuf = self._ln2_bufs(fq)
        L = t["lnb"]
        z = buf[:, ts * 1024:(ts + 1) * 1024]
        scr = obuf[:, ts * 1024:(ts + 1) * 1024]
        self.w("scalar", "s_fu", self.c["s_fu"])
        if self.on("scalar"):
            self.em(self.e.activation(out=scr, in_=z, func=AF.Square,
                                      bias=0.0, scale=1.0,
                                      accum_out=L[:, 32 + ts:33 + ts]))
        self.i("scalar", "s_l2", 1)
        if self.on("scalar"):
            self.em(self.e.activation(out=scr, in_=z, func=AF.Copy,
                                      bias=0.0, scale=1.0,
                                      accum_out=L[:, 36 + ts:37 + ts]))
        self.i("scalar", "s_lnc", 1)

    def ln2_chain(self, fq):
        t = self.t
        self.w("scalar", "s_lnc", self.c["s_lnc"])
        self.w("scalar", "s_l2", self.c["s_l2"])
        if self.on("scalar"):
            sc = self.e
            L = t["lnb"]
            sc.activation(out=L[:, 40:44], in_=L[:, 36:40], func=AF.Copy,
                          bias=0.0, scale=1.0 / 1024.0)
            self.em(sc.activation(out=L[:, 44:48], in_=L[:, 36:40],
                                  func=AF.Square, bias=0.0,
                                  scale=1.0 / 1024.0))
        self.i("scalar", "s_lnc", 1)
        self.w("scalar", "s_lnc", self.c["s_lnc"])
        if self.on("scalar"):
            self.em(self.e.activation(out=t["lnb"][:, 48:52],
                                      in_=t["lnb"][:, 44:48], func=AF.Copy,
                                      bias=1e-5, scale=-1.0))
        self.i("scalar", "s_lnc", 1)
        self.w("scalar", "s_lnc", self.c["s_lnc"])
        if self.on("scalar"):
            sc = self.e
            L = t["lnb"]
            ins = None
            for ts2 in range(4):
                ins = sc.activation(out=L[:, 52 + ts2:53 + ts2],
                                    in_=L[:, 32 + ts2:33 + ts2], func=AF.Ln,
                                    bias=L[:, 48 + ts2:49 + ts2],
                                    scale=t["inv1024"][:])
            self.em(ins)
        self.i("scalar", "s_lnc", 1)
        self.w("scalar", "s_lnc", self.c["s_lnc"])
        if self.on("scalar"):
            self.em(self.e.activation(out=t["lnb"][:, 56:60],
                                      in_=t["lnb"][:, 52:56], func=AF.Exp,
                                      bias=0.0, scale=t["mhalf"][:]))
        self.i("scalar", "s_lnc", 1)

    def ln2_apply(self, fq, ts):
        t = self.t
        buf, obuf = self._ln2_bufs(fq)
        self.w("vector", "s_lnc", self.c["s_lnc"])
        if self.on("vector"):
            self.em(self.e.tensor_scalar(
                out=obuf[:, ts * 1024:(ts + 1) * 1024],
                in0=buf[:, ts * 1024:(ts + 1) * 1024],
                scalar1=t["lnb"][:, 40 + ts:41 + ts],
                scalar2=t["lnb"][:, 56 + ts:57 + ts],
                op0=ALU.subtract, op1=ALU.mult))
        self.i("vector", "s_l2", 1)
        self.w("sync", "s_l2", self.c["s_l2"])
        if self.on("sync"):
            row = fq * 512 + ts * 128
            self.em(self.e.dma_start(
                out=t["dout"].ap()[row:row + 128, :],
                in_=obuf[:, ts * 1024:(ts + 1) * 1024]))
        self.i("sync", "m_out", 16)

    def mix_step(self, fq, which, ch):
        t = self.t
        mu = t["mu2"][:, 0:8] if which == "r" else t["mu2"][:, 16:24]
        mui = t["mu2"][:, 8:16] if which == "r" else t["mu2"][:, 24:32]
        xs = XF2SLOT[fq] * 4096
        ps = XF2SLOT[fq - 1 if fq >= 1 else 7] * 4096
        self.w("vector", "s_fp", self.c["s_fp"])
        if ch == 0 and which == "k":
            self.w("vector", "s_pro", self.c["s_pro"])
        if ch == 0 and which == "r" and fq == 0:
            self.w("vector", "m_sp", 32)
        if self.on("vector"):
            v = self.e
            xsl = t["xf2"][:, xs + ch * 512:xs + (ch + 1) * 512]
            v.tensor_scalar(out=t["fstage"][:, 0:512], in0=xsl,
                            scalar1=mu[:, ch:ch + 1], scalar2=None,
                            op0=ALU.mult)
            if fq >= 1:
                prev = t["xf2"][:, ps + ch * 512:ps + (ch + 1) * 512]
                ins = v.scalar_tensor_tensor(
                    out=t["xrk"][:, ch * 512:(ch + 1) * 512], in0=prev,
                    scalar=mui[:, ch:ch + 1], in1=t["fstage"][:, 0:512],
                    op0=ALU.mult, op1=ALU.add)
            else:
                prev = t["xf2"][:, ps + ch * 512:ps + ch * 512 + 511]
                v.scalar_tensor_tensor(
                    out=t["xrk"][:, ch * 512 + 1:(ch + 1) * 512], in0=prev,
                    scalar=mui[:, ch:ch + 1], in1=t["fstage"][:, 1:512],
                    op0=ALU.mult, op1=ALU.add)
                ins = v.tensor_copy(t["xrk"][:, ch * 512:ch * 512 + 1],
                                    t["fstage"][:, 0:1])
            self.em(ins)
        self.i("vector", "s_mix", 1)

    def ffn_mm_group(self, wname, g, rhs):
        t = self.t
        wi = {"wr": 0, "wk": 1, "wv": 2}[wname]
        seq = self.c["m_wrl"] // 16
        slot = seq % 5
        if seq >= 5:
            self.w("sync", "s_fp", self.grp_fp[seq - 5])
        self.w("sync", "m_wst", 16 * 24)
        if self.on("sync"):
            off = (wi * 8 + g) * (128 * 8 * 128)
            src = bass.AP(t["dwscr"], off, [[1024, 128], [1, 1024]])
            self.em(self.e.dma_start(
                out=t["wring"][:, slot * 1024:(slot + 1) * 1024], in_=src))
        self.i("sync", "m_wrl", 16)
        self.w("tensor", "m_wrl", 16 * (seq + 1))
        self.w("tensor", "s_fu", self.c["s_fu"])
        self.w("tensor", "s_mix", self.c["s_mix"])
        if self.on("tensor"):
            tn = self.e
            ins = None
            for dm in range(8):
                ins = tn.matmul(t["psB7"][:],
                                t["wring"][:, slot * 1024 + dm * 128:
                                           slot * 1024 + (dm + 1) * 128],
                                rhs[:, dm * 512:(dm + 1) * 512],
                                start=(dm == 0), stop=(dm == 7))
            self.em(ins)
        self.i("tensor", "s_fp", 1)
        self.grp_fp.append(self.c["s_fp"])

    def ffn_zero(self):
        t = self.t
        self.w("gpsimd", "s_fu", self.c["s_fu"])
        self.w("gpsimd", "s_l2", self.c["s_l2"])
        self.w("gpsimd", "s_lnc", self.c["s_lnc"])
        self.w("gpsimd", "m_out", self.c["m_out"])
        if self.on("gpsimd"):
            self.em(self.e.dma_start(out=t["s1"][:], in_=t["ds1scr"].ap()))
        self.i("gpsimd", "m_sp", 16)
        self.ffn_enqueue(0)
        self.pump_all()

    def finale(self):
        oq = "sync"
        for name in self.cfg.get("dbg", []):
            for sn in ("s_eps", "s_ex", "s_ln", "s_l2", "s_xfe", "s_pve",
                       "s_epr", "s_mix", "s_pro", "s_zt"):
                self.w(oq, sn, self.c[sn])
            if self.on(oq):
                self.em(self.e.dma_start(out=self.t["dbg_" + name].ap(),
                                         in_=self.t[name][:]))
            self.i(oq, "m_out", 16)
        if self.on(oq) and self.c["m_out"] > 0:
            self.e.wait_ge(self.sem["m_out"], self.c["m_out"])


def _build_nc(cfg=None):
    nc = bass.Bass("TRN2", target_bir_lowering=False)
    t = {}
    t["dx"] = nc.dram_tensor("x", [T, D], F32, kind="ExternalInput")
    t["dwr"] = nc.dram_tensor("wrT", [D, D], F32, kind="ExternalInput")
    t["dwk"] = nc.dram_tensor("wkT", [D, D], F32, kind="ExternalInput")
    t["dwv"] = nc.dram_tensor("wvT", [D, D], F32, kind="ExternalInput")
    t["dmur"] = nc.dram_tensor("mur", [D], F32, kind="ExternalInput")
    t["dmuk"] = nc.dram_tensor("muk", [D], F32, kind="ExternalInput")
    t["dipw"] = nc.dram_tensor("ipw", [128, 1], F32, kind="ExternalInput")
    t["dout"] = nc.dram_tensor("out", [T, D], F32, kind="ExternalOutput")
    t["dwscr"] = nc.dram_tensor("wscr", [3 * 8 * 128 * 8 * 128], BF16,
                                kind="Internal")
    t["ds1scr"] = nc.dram_tensor("s1scr", [128, 4096], F32, kind="Internal")
    t["dcrow"] = nc.dram_tensor("crow", [128], F32, kind="Internal")

    es = ExitStack()
    sb = [
        ("xt", [128, NTB * XTB], BF16),
        ("xf", [128, 8192], BF16),
        ("xf2", [128, 16384], BF16),
        ("pun", [128, 3072], BF16),
        ("pvraw", [128, 4096], F32),
        ("s1", [128, 4096], F32),
        ("xin", [128, 4096], F32),
        ("rbuf", [128, 4096], FP16),
        ("k_act", [128, 4096], BF16),
        ("xrk", [128, 4096], BF16),
        ("wring", [128, 5120], BF16),
        ("fstage", [128, 1024], F32),
        ("t1st", [128, 512], FP16),
        ("recipz", [128, 64], F32),
        ("idb", [128, 128], BF16),
        ("idf", [128, 128], F32),
        ("mu2", [128, 32], F32),
        ("negc", [128, 1], F32),
        ("wsc", [128, 1], F32),
        ("scl_i", [128, 1], F32),
        ("m1", [128, 1], F32),
        ("epsap", [128, 1], F32),
        ("inv1024", [128, 1], F32),
        ("mhalf", [128, 1], F32),
        ("one1", [128, 1], F32),
        ("lnz", [128, 8], F32),
        ("c128", [128, 1], F32),
        ("lna", [128, 4], F32),
        ("lnb", [128, 64], F32),
        ("ipw1", [128, 1], F32),
        ("c11", [1, 4], F32),
    ]
    for name, shape, dt in sb:
        t[name] = es.enter_context(nc.sbuf_tensor(name, shape, dt))
    t["s1b"] = t["xrk"]
    t["zst2"] = t["fstage"]
    t["onesf"] = t["fstage"].ap()[0:1, 512:640]
    t["crow"] = t["fstage"].ap()[0:1, 640:768]
    t["m8row"] = t["fstage"].ap()[0:1, 768:896]
    for name, shape, dt in [
        ("psS", [128, 2048], F32),
        ("psPV0", [128, 512], F32),
        ("psPV1", [128, 512], F32),
        ("psXF", [128, 1024], BF16),
        ("psB7", [128, 512], F32),
    ]:
        t[name] = es.enter_context(nc.psum_tensor(name, shape, dt))

    if (cfg or {}).get("dump_y_qb", -1) >= 0:
        t["dbg_ydump"] = nc.dram_tensor("dbg_ydump", [512, 1024], F32,
                                        kind="ExternalOutput")
    for name in (cfg or {}).get("dbg", []):
        th = t[name]
        shape = list(th.shape) if hasattr(th, "shape") else None
        t["dbg_" + name] = nc.dram_tensor("dbg_" + name, shape, th.dtype,
                                          kind="ExternalOutput")
    sem = {}
    for name in SEMS:
        sem[name] = es.enter_context(nc.semaphore(name))

    with es, nc.Block() as block:
        def runner(role):
            def f(eng):
                with nc.allow_low_precision("bf16 matmul-operand staging"):
                    W(role, eng, t, sem, cfg).walk()
            return f

        block.sync(runner("sync"))
        block.gpsimd(runner("gpsimd"))
        block.tensor(runner("tensor"))
        block.scalar(runner("scalar"))
        block.vector(runner("vector"))
    return nc


_PI = None


def _perms():
    global _PI
    if _PI is None:
        tt = np.arange(T)
        pi = (tt % 8) * 512 + tt // 8
        inv = (tt % 512) * 8 + tt // 512
        _PI = (pi, inv)
    return _PI


def kernel(L0_attn_x, L0_attn_inter_palace_weight, L0_ffn_W_r, L0_ffn_W_k,
           L0_ffn_W_v, L0_ffn_mu_r, L0_ffn_mu_k):
    x = np.asarray(L0_attn_x, np.float32)
    ipw = np.asarray(L0_attn_inter_palace_weight, np.float32).reshape(1, 1)
    wrT = np.ascontiguousarray(np.asarray(L0_ffn_W_r, np.float32).T)
    wkT = np.ascontiguousarray(np.asarray(L0_ffn_W_k, np.float32).T)
    wvT = np.ascontiguousarray(np.asarray(L0_ffn_W_v, np.float32).T)
    mur = np.ascontiguousarray(np.asarray(L0_ffn_mu_r, np.float32))
    muk = np.ascontiguousarray(np.asarray(L0_ffn_mu_k, np.float32))

    pi, inv = _perms()
    nc = _build_nc()
    in_maps = []
    for b in range(NCORE):
        in_maps.append({
            "x": np.ascontiguousarray(x[b][inv]),
            "wrT": wrT, "wkT": wkT, "wvT": wvT,
            "mur": mur, "muk": muk,
            "ipw": np.full((128, 1), ipw[0, 0], np.float32),
        })
    res = run_bass_kernel_spmd(nc, in_maps, core_ids=list(range(NCORE)))
    out = np.empty((NCORE, T, D), np.float32)
    for b in range(NCORE):
        out[b] = np.asarray(res.results[b]["out"], np.float32)[pi]
    return out


if __name__ == "__main__":
    _build_nc()
    print("built ok")
